# version 07
# baseline (speedup 1.0000x reference)
import numpy as np
import ml_dtypes
import concourse.bass as bass
import concourse.mybir as mybir
from concourse.bass_utils import run_bass_kernel_spmd

F32 = mybir.dt.float32
BF16 = mybir.dt.bfloat16
U8 = mybir.dt.uint8
AF = mybir.ActivationFunctionType
ALU = mybir.AluOpType

B, S, H = 8, 4096, 2048
_BC = [0]
I = 8192
NH, DK = 16, 128
T = 512
NCHUNK = S // T
NTT = T // 128
NKB = H // 128
NIT = I // 128
NI16 = 16
EPS = 1e-5
QSCALE = float(DK) ** -0.5
SEQ_DEBUG_FLAG = False
STOP_AFTER = None

ENGINES = ("sync", "scalar", "gpsimd", "vector", "tensor")
DMA_ENG = ("sync", "scalar", "gpsimd")


class Res:
    __slots__ = ("name", "writers", "readers", "frozen")

    def __init__(self, name, frozen=False):
        self.name = name
        self.writers = {}
        self.readers = {}
        self.frozen = frozen


class Prog:

    def __init__(self):
        self.items = {e: [] for e in ENGINES}
        self.needed = {e: set() for e in ENGINES}
        self.cur_tag = "init"

    def tag(self, t):
        self.cur_tag = t

    def emit(self, engine, fn, reads=(), writes=(), dma=False, batch=None):
        idx = len(self.items[engine])
        st = ("q:" + engine) if dma else engine
        deps = {}

        def add(s2, i2):
            if s2 == st and (dma or engine == "tensor"):
                return
            if i2 > deps.get(s2, -1):
                deps[s2] = i2

        for r in reads:
            for s2, i2 in r.writers.items():
                add(s2, i2)
        for w in writes:
            for s2, i2 in w.writers.items():
                add(s2, i2)
            for s2, i2 in w.readers.items():
                add(s2, i2)
        for r in reads:
            if not r.frozen and r.readers.get(st, -1) < idx:
                r.readers[st] = idx
        for w in writes:
            w.writers[st] = idx
            w.readers = {}
        self.items[engine].append((fn, deps, dma, self.cur_tag, batch))

    def finalize(self, nc):
        dma_index = {}
        for e in ENGINES:
            seq = 0
            di = []
            for (fn, deps, dma, tg, batch) in self.items[e]:
                di.append(seq if dma else -1)
                if dma:
                    seq += 1
            dma_index[e] = di
        batch_end = {}
        for e in ENGINES:
            items = self.items[e]
            ends = [-1] * len(items)
            groups = []
            for i, (fn, deps, dma, tg, batch) in enumerate(items):
                if not dma:
                    continue
                if groups and batch is not None and groups[-1][0] is not None and groups[-1][0] == batch:
                    groups[-1][1].append(i)
                else:
                    groups.append([batch, [i]])
            for key, idxs in groups:
                endseq = dma_index[e][idxs[-1]]
                for i in idxs:
                    ends[i] = endseq
            batch_end[e] = ends
        for e in ENGINES:
            for (fn, deps, dma, tg, batch) in self.items[e]:
                for s2, i2 in deps.items():
                    if not s2.startswith("q:"):
                        self.needed[s2].add(i2)
        mstone = {}
        for e in ENGINES:
            for n, i in enumerate(sorted(self.needed[e])):
                mstone[(e, i)] = n + 1
        total_cmp = {e: len(self.needed[e]) for e in ENGINES}
        total_dma = {e: sum(1 for d in dma_index[e] if d >= 0) * 16 for e in ENGINES}
        for e in ENGINES:
            assert total_cmp[e] < 60000, (e, total_cmp[e])

        with nc.Block() as block:
            sems = {}
            for e in ENGINES:
                sems[("cmp", e)] = nc.semaphore(f"p_{e}").__enter__()
                if e in DMA_ENG:
                    sems[("dma", e)] = nc.semaphore(f"q_{e}").__enter__()

            def make(e):
                def emitf(eng):
                    floors = {}
                    for idx, (fn, deps, dma, tg, batch) in enumerate(self.items[e]):
                        if dma and dma_index[e][idx] >= 1:
                            myseq = dma_index[e][idx]
                            myend = batch_end[e][idx]
                            if not hasattr(self, "_seq_end"):
                                self._seq_end = {}
                            se = self._seq_end.get(e)
                            if se is None:
                                se = [None] * (sum(1 for d in dma_index[e] if d >= 0))
                                for j, (fn2, d2, dm2, tg2, b2) in enumerate(self.items[e]):
                                    if dm2:
                                        se[dma_index[e][j]] = batch_end[e][j]
                                self._seq_end[e] = se
                            sstart = myseq
                            while sstart > 0 and se[sstart - 1] == myend:
                                sstart -= 1
                            if sstart >= 1:
                                key, val = ("dma", e), sstart * 16
                                if floors.get(key, 0) < val:
                                    eng.wait_ge(sems[key], val)
                                    floors[key] = val
                        for s2 in sorted(deps):
                            i2 = deps[s2]
                            if s2.startswith("q:"):
                                e2 = s2[2:]
                                key, val = ("dma", e2), (batch_end[e2][i2] + 1) * 16
                            else:
                                key, val = ("cmp", s2), mstone[(s2, i2)]
                            if floors.get(key, 0) < val:
                                eng.wait_ge(sems[key], val)
                                floors[key] = val
                        with nc.named_scope(tg):
                            ins = fn(eng)
                        if dma:
                            ins.then_inc(sems[("dma", e)], 16)
                        elif idx in self.needed[e]:
                            ins.then_inc(sems[("cmp", e)], 1)
                    if e == "sync":
                        for e2 in ENGINES:
                            if total_cmp[e2]:
                                eng.wait_ge(sems[("cmp", e2)], total_cmp[e2])
                            if e2 in DMA_ENG and total_dma[e2]:
                                eng.wait_ge(sems[("dma", e2)], total_dma[e2])
                return emitf

            block.sync(make("sync"))
            block.scalar(make("scalar"))
            block.gpsimd(make("gpsimd"))
            block.vector(make("vector"))
            block.tensor(make("tensor"))


def build_program(nchunk=NCHUNK):
    nc = bass.Bass("TRN2", target_bir_lowering=False)

    x_d = nc.dram_tensor("x", [S, H], F32, kind="ExternalInput")
    wq_d = nc.dram_tensor("wq", [H, H], F32, kind="ExternalInput")
    wf_d = nc.dram_tensor("wf", [H, H], F32, kind="ExternalInput")
    wi_d = nc.dram_tensor("wi", [H, H], F32, kind="ExternalInput")
    wo_d = nc.dram_tensor("wo", [H, H], F32, kind="ExternalInput")
    gw_d = nc.dram_tensor("gw", [H], F32, kind="ExternalInput")
    wg_d = nc.dram_tensor("wg", [H, I], F32, kind="ExternalInput")
    wu_d = nc.dram_tensor("wu", [H, I], F32, kind="ExternalInput")
    wd_d = nc.dram_tensor("wd", [I, H], F32, kind="ExternalInput")
    y_d = nc.dram_tensor("y", [S, H], F32, kind="ExternalOutput")

    wgt = nc.dram_tensor("wgt", [NI16, NKB, 128, 512], BF16)
    wut = nc.dram_tensor("wut", [NI16, NKB, 128, 512], BF16)
    wdt = nc.dram_tensor("wdt", [NIT, 128, H], BF16)
    wqt = nc.dram_tensor("wqt", [NH, NKB, 128, 128], BF16)
    wft = nc.dram_tensor("wft", [NH, NKB, 128, 128], BF16)
    wit = nc.dram_tensor("wit", [4, NKB, 128, 512], BF16)
    wot = nc.dram_tensor("wot", [2, NKB, 128, 1024], BF16)

    ident_d = nc.inline_tensor(np.eye(128).astype(ml_dtypes.bfloat16), name="ident_c")
    mask_np = (np.arange(128)[:, None] <= np.arange(128)[None, :]).astype(np.uint8)
    mask_d = nc.inline_tensor(mask_np, name="maskc_c")

    p = Prog()
    from contextlib import ExitStack
    stack = ExitStack()
    with stack:
        sbt = lambda name, shape, dt: stack.enter_context(nc.sbuf_tensor(name, shape, dt))
        x_res = sbt("x_res", [128, NTT * H], F32)
        x_ln = sbt("x_ln", [128, NKB * 512], BF16)
        Hst = sbt("Hst", [128, NH * 128], F32)
        wringA = sbt("wringA", [128, 16384], BF16)
        wringB = sbt("wringB", [128, 16384], BF16)
        hsp = sbt("hsp", [128, 32768], BF16)
        scr = sbt("scr", [128, 3072], F32)
        sbf = sbt("sbf", [128, 2048], BF16)
        smb = sbt("smb", [128, 1024], BF16)
        hstage = sbt("hstage", [128, 2048], BF16)
        ident = sbt("ident", [128, 128], BF16)
        masku = sbt("masku", [128, 128], U8)
        zerosb = sbt("zerosb", [128, 128], BF16)
        ones = sbt("ones", [128, 128], F32)
        small = sbt("small", [128, 64], F32)
        bnscr = sbt("bnscr", [128, 32], F32)
        gwcol = sbt("gwcol", [128, NKB], F32)
        constb = sbt("constb", [128, 2], F32)
        pb = [stack.enter_context(nc.psum_tensor(f"pb{i}", [128, 512], F32)) for i in range(8)]

        arena32 = hsp[:].bitcast(F32)

        def siluq_ap(c0, n):
            return arena32[:, c0:c0 + n]

        def V_ap(c0, n):
            return hsp[:, 16384 + c0:16384 + c0 + n]

        def o_ap(c0, n):
            return hsp[:, 24576 + c0:24576 + c0 + n]

        woscr = arena32[:, 14336:16384]
        wosbf = hsp[:, 26624:28672]

        R_bank = [Res(f"bank{i}") for i in range(8)]
        R_xres = [Res(f"xres{t}") for t in range(NTT)]
        R_xln = Res("xln")
        R_arena = Res("arena")
        R_H = [Res(f"H{h}") for h in range(NH)]
        R_ring = {"A": Res("ringA"), "B": Res("ringB")}
        R_scr = Res("scr")
        R_sbf = Res("sbf")
        R_smb = Res("smb")
        R_sm = [Res(f"sm{i}") for i in range(4)]
        R_ktm = [Res(f"ktm{i}") for i in range(2)]
        R_ht = [Res(f"ht{i}") for i in range(2)]
        R_hstage = Res("hstage")
        R_small = Res("small")
        R_small_s = [Res("small0"), Res("small1")]
        R_bn = Res("bn")
        R_const = Res("const", frozen=True)
        R_wg = [Res(f"wg{i}") for i in range(NI16)]
        R_wu = [Res(f"wu{i}") for i in range(NI16)]
        R_wd = [Res(f"wd{i}") for i in range(8)]
        R_wq = Res("wqt")
        R_wf = Res("wft")
        R_wi = Res("wit")
        R_wo = Res("wot")
        R_gw = Res("gw")
        R_y = Res("y")

        p.emit("scalar", lambda e: e.dma_start(out=ident[:], in_=ident_d[:]),
               writes=[R_const], dma=True)
        p.emit("scalar", lambda e: e.dma_start(out=masku[:], in_=mask_d[:]),
               writes=[R_const], dma=True)
        p.emit("vector", lambda e: e.memset(zerosb[:], 0.0), writes=[R_const])
        for sg4 in range(4):
            p.emit("vector", lambda e, sg4=sg4: e.memset(smb[:, sg4 * 128:(sg4 + 1) * 128], 0.0),
                   writes=[R_sm[sg4]])
        p.emit("vector", lambda e: e.memset(constb[:, 0:1], EPS), writes=[R_const])
        p.emit("vector", lambda e: e.memset(constb[:, 1:2], 1.0), writes=[R_const])
        nc.const_aps.register(F32, EPS, constb[:, 0:1])
        nc.const_aps.register(F32, 1.0, constb[:, 1:2])
        p.emit("vector", lambda e: e.memset(ones[:], 1.0), writes=[R_const])
        p.emit("vector", lambda e: e.memset(Hst[:], 0.0), writes=list(R_H))
        for kb in range(NKB):
            src = bass.AP(gw_d, kb * 128, [[1, 128], [0, 1]])
            p.emit("scalar", lambda e, s=src, kb=kb: e.dma_start(
                out=gwcol[:, kb:kb + 1], in_=s, allow_slow_non_contiguous=True),
                writes=[R_gw], dma=True)

        wg4 = wg_d[:].rearrange("(kb p) (ib c) -> ib kb p c", p=128, c=512)
        wu4 = wu_d[:].rearrange("(kb p) (ib c) -> ib kb p c", p=128, c=512)
        for ib in range(NI16):
            p.emit("gpsimd", lambda e, ib=ib: e.dma_start(out=wgt[ib], in_=wg4[ib]),
                   writes=[R_wg[ib]], dma=True, batch=f"pcgu{ib // 4}")
            p.emit("gpsimd", lambda e, ib=ib: e.dma_start(out=wut[ib], in_=wu4[ib]),
                   writes=[R_wu[ib]], dma=True, batch=f"pcgu{ib // 4}")
        wd3 = wd_d[:].rearrange("(it p) h -> it p h", p=128)
        for run in range(8):
            p.emit("gpsimd", lambda e, run=run: e.dma_start(
                out=wdt[run * 8:(run + 1) * 8], in_=wd3[run * 8:(run + 1) * 8]),
                writes=[R_wd[run]], dma=True, batch=f"pcwd{run // 4}")
        for kb in range(NKB):
            s1 = wq_d[kb * 128:(kb + 1) * 128, :].rearrange("p (dt m) -> dt p m", m=128)
            p.emit("gpsimd", lambda e, s=s1, kb=kb: e.dma_start(out=wqt[:, kb], in_=s),
                   writes=[R_wq], dma=True, batch=f"pca{kb // 2}")
            s2 = wf_d[kb * 128:(kb + 1) * 128, :].rearrange("p (dt m) -> dt p m", m=128)
            p.emit("gpsimd", lambda e, s=s2, kb=kb: e.dma_start(out=wft[:, kb], in_=s),
                   writes=[R_wf], dma=True, batch=f"pca{kb // 2}")
            s3 = wi_d[kb * 128:(kb + 1) * 128, :].rearrange("p (vb c) -> vb p c", c=512)
            p.emit("gpsimd", lambda e, s=s3, kb=kb: e.dma_start(out=wit[:, kb], in_=s),
                   writes=[R_wi], dma=True, batch=f"pca{kb // 2}")
        for kb in range(NKB):
            p.emit("scalar", lambda e, kb=kb: e.dma_start(
                out=woscr, in_=wo_d[kb * 128:(kb + 1) * 128, :]),
                reads=[R_arena], writes=[R_arena], dma=True)
            p.emit("vector", lambda e, kb=kb: e.tensor_scalar_mul(
                wosbf, woscr, gwcol[:, kb:kb + 1]),
                reads=[R_arena, R_gw], writes=[R_arena])
            src = wosbf.rearrange("p (dh c) -> p dh c", c=1024)
            dsto = wot[:, kb].rearrange("dh p c -> p dh c")
            p.emit("scalar", lambda e, s=src, d=dsto: e.dma_start(out=d, in_=s),
                   reads=[R_arena], writes=[R_wo], dma=True)

        def mm(bank, lhsT, rhs, start, stop, reads, group=None):
            n = rhs.shape[-1]
            w = [R_bank[bank]] + ([group] if group is not None else [])
            p.emit("tensor", lambda e, b=bank, l=lhsT, r=rhs, st=start, sp=stop, n=n:
                   e.matmul(pb[b][:, 0:n], l, r, start=st, stop=sp),
                   reads=list(reads), writes=w)

        def tp(bank, in_ap, reads, group=None):
            out = pb[bank][:, 0:64].bitcast(BF16)
            w = [R_bank[bank]] + ([group] if group is not None else [])
            p.emit("tensor", lambda e, o=out, i=in_ap: e.transpose(o, i, ident[:]),
                   reads=list(reads) + [R_const], writes=w)
            return out

        R_grp = [Res(f"grp{i}") for i in range(4)]

        def ln():
            for tt in range(NTT):
                xs = x_res[:, tt * H:(tt + 1) * H]
                for q in range(4):
                    p.emit("vector", lambda e, q=q, xs=xs: e.bn_stats(
                        bnscr[:, q * 6:q * 6 + 6], xs[:, q * 512:(q + 1) * 512]),
                        reads=[R_xres[tt]], writes=[R_bn])
                p.emit("vector", lambda e: e.bn_aggr(bnscr[:, 24:26], bnscr[:, 0:24]),
                       reads=[R_bn], writes=[R_bn])
                p.emit("scalar", lambda e: e.activation(small[:, 0:1], bnscr[:, 25:26], AF.Ln, bias=EPS),
                       reads=[R_bn], writes=[R_small])
                p.emit("scalar", lambda e: e.activation(small[:, 1:2], small[:, 0:1], AF.Exp, scale=-0.5),
                       reads=[R_small], writes=[R_small])
                p.emit("vector", lambda e, xs=xs: e.tensor_scalar(
                    sbf[:], xs, bnscr[:, 24:25], small[:, 1:2], ALU.subtract, ALU.mult),
                    reads=[R_xres[tt], R_bn, R_small], writes=[R_sbf])
                for q4 in range(4):
                    grp = R_grp[q4 % 2]
                    outs = []
                    for j in range(4):
                        kb = q4 * 4 + j
                        outs.append((4 + j, tp(4 + j, sbf[:, kb * 128:(kb + 1) * 128],
                                               [R_sbf], group=grp)))
                    for j in range(4):
                        kb = q4 * 4 + j
                        bk, o_ps = outs[j]
                        dst = x_ln[:, kb * 512 + tt * 128: kb * 512 + (tt + 1) * 128]
                        p.emit("vector", lambda e, o=o_ps, d=dst: e.tensor_copy(d, o),
                               reads=[R_bank[bk], grp], writes=[R_xln])

        def ffn():
            pend = []

            def flush_tp(bankset):
                nonlocal pend
                for (par, tt, i16) in pend:
                    grp = R_grp[2 + (tt % 2)]
                    outs = []
                    for j2 in range(4):
                        bk = bankset[j2 % 4]
                        src = hstage[:, par * 1024 + (tt % 2) * 512 + j2 * 128:
                                     par * 1024 + (tt % 2) * 512 + (j2 + 1) * 128]
                        outs.append((bk, tp(bk, src, [R_hstage], group=grp)))
                    for j2 in range(4):
                        it = i16 * 4 + j2
                        bk, o_ps = outs[j2]
                        dst = hsp[:, it * 512 + tt * 128: it * 512 + (tt + 1) * 128]
                        p.emit("vector", lambda e, o=o_ps, d=dst: e.tensor_copy(d, o),
                               reads=[R_bank[bk], grp], writes=[R_arena])
                pend = []

            it_count = 0
            base_tag = p.cur_tag
            p.tag(base_tag + "gu")
            for i16 in range(NI16):
                ring = "A" if i16 % 2 == 0 else "B"
                gbuf = wringA if ring == "A" else wringB
                dstg = gbuf[:, 0:8192].rearrange("p (kb c) -> p kb c", c=512)
                srcg = wgt[i16].rearrange("kb p c -> p kb c")
                _BC[0] += 1
                p.emit("sync", lambda e, d=dstg, s=srcg: e.dma_start(out=d, in_=s),
                       reads=[R_wg[i16]], writes=[R_ring[ring]], dma=True, batch=f"gu{_BC[0]}")
                dstu = gbuf[:, 8192:16384].rearrange("p (kb c) -> p kb c", c=512)
                srcu = wut[i16].rearrange("kb p c -> p kb c")
                p.emit("sync", lambda e, d=dstu, s=srcu: e.dma_start(out=d, in_=s),
                       reads=[R_wu[i16]], writes=[R_ring[ring]], dma=True, batch=f"gu{_BC[0]}")
                for pair in range(2):
                    par = it_count % 2
                    bs = [0, 1, 2, 3] if par == 0 else [4, 5, 6, 7]
                    obs = [4, 5, 6, 7] if par == 0 else [0, 1, 2, 3]
                    tts = (2 * pair, 2 * pair + 1)
                    pgrp = R_grp[par]
                    for kb in range(NKB):
                        for j, tt in enumerate(tts):
                            lh = x_ln[:, kb * 512 + tt * 128: kb * 512 + (tt + 1) * 128]
                            mm(bs[j], lh, gbuf[:, kb * 512:(kb + 1) * 512],
                               kb == 0, kb == NKB - 1, [R_xln, R_ring[ring]], group=pgrp)
                            mm(bs[2 + j], lh, gbuf[:, 8192 + kb * 512:8192 + (kb + 1) * 512],
                               kb == 0, kb == NKB - 1, [R_xln, R_ring[ring]], group=pgrp)
                    flush_tp(obs)
                    for j, tt in enumerate(tts):
                        p.emit("scalar", lambda e, b=bs[j]: e.activation(
                            scr[:, 0:512], pb[b][:], AF.Silu),
                            reads=[R_bank[bs[j]], pgrp], writes=[R_scr])
                        p.emit("vector", lambda e, b=bs[2 + j]: e.tensor_mul(
                            scr[:, 512:1024], scr[:, 0:512], pb[b][:]),
                            reads=[R_bank[bs[2 + j]], R_scr], writes=[R_scr])
                        hh = scr[:, 512:1024]
                        p.emit("scalar", lambda e: e.activation(scr[:, 1024:1536], hh, AF.Abs),
                               reads=[R_scr], writes=[R_scr])
                        a2 = scr[:, 1024:1536].rearrange("p (g f) -> p g f", f=2)
                        p.emit("vector", lambda e, a2=a2: e.tensor_tensor(
                            scr[:, 1536:1792], a2[:, :, 0], a2[:, :, 1], ALU.max),
                            reads=[R_scr], writes=[R_scr])
                        p.emit("vector", lambda e, a2=a2: e.tensor_tensor(
                            scr[:, 1792:2048], a2[:, :, 0], a2[:, :, 1], ALU.min),
                            reads=[R_scr], writes=[R_scr])
                        mx2 = scr[:, 1536:1792].rearrange("p (g f) -> p g f", f=2)
                        mn2 = scr[:, 1792:2048].rearrange("p (g f) -> p g f", f=2)
                        p.emit("vector", lambda e, v=mx2: e.tensor_tensor(
                            scr[:, 2048:2176], v[:, :, 0], v[:, :, 1], ALU.min),
                            reads=[R_scr], writes=[R_scr])
                        p.emit("vector", lambda e, v=mn2: e.tensor_tensor(
                            scr[:, 2176:2304], v[:, :, 0], v[:, :, 1], ALU.max),
                            reads=[R_scr], writes=[R_scr])
                        p.emit("vector", lambda e: e.tensor_tensor(
                            scr[:, 2304:2432], scr[:, 2048:2176], scr[:, 2176:2304], ALU.max),
                            reads=[R_scr], writes=[R_scr])
                        a3 = scr[:, 1024:1536].rearrange("p (g f) -> p g f", f=4)
                        g3 = scr[:, 2560:3072].rearrange("p (g f) -> p g f", f=4)
                        for m4 in range(4):
                            p.emit("vector", lambda e, m4=m4, a3=a3, g3=g3: e.tensor_tensor(
                                g3[:, :, m4], a3[:, :, m4], scr[:, 2304:2432], ALU.is_ge),
                                reads=[R_scr], writes=[R_scr])
                        dsth = hstage[:, par * 1024 + (tt % 2) * 512: par * 1024 + (tt % 2) * 512 + 512]
                        p.emit("vector", lambda e, hh=hh, d=dsth: e.scalar_tensor_tensor(
                            d, hh, 0.5, scr[:, 2560:3072], ALU.mult, ALU.mult),
                            reads=[R_scr], writes=[R_hstage])
                        pend.append((par, tt, i16))
                    it_count += 1
            flush_tp([0, 1, 2, 3])
            p.tag(base_tag + "dn")
            for dh in range(2):
                for run in range(8):
                    ring = "A" if run % 2 == 0 else "B"
                    rbuf = wringA if ring == "A" else wringB
                    dst = rbuf[:, 0:8 * H].rearrange("p (it h) -> p it h", h=H)
                    src = wdt[run * 8:(run + 1) * 8].rearrange("it p h -> p it h")
                    p.emit("sync", lambda e, d=dst, s=src: e.dma_start(out=d, in_=s),
                           reads=[R_wd[run]], writes=[R_ring[ring]], dma=True)
                    for kk in range(8):
                        it = run * 8 + kk
                        for tt in range(NTT):
                            lh = hsp[:, it * 512 + tt * 128: it * 512 + (tt + 1) * 128]
                            for dblk in range(2):
                                rh = rbuf[:, kk * H + dh * 1024 + dblk * 512:
                                          kk * H + dh * 1024 + (dblk + 1) * 512]
                                mm(tt * 2 + dblk, lh, rh, it == 0, it == NIT - 1,
                                   [R_arena, R_ring[ring]])
                for tt in range(NTT):
                    for dblk in range(2):
                        dst = x_res[:, tt * H + dh * 1024 + dblk * 512:
                                    tt * H + dh * 1024 + (dblk + 1) * 512]
                        bk = tt * 2 + dblk
                        p.emit("vector", lambda e, d=dst, bk=bk: e.tensor_add(d, d, pb[bk][:]),
                               reads=[R_bank[bk], R_xres[tt]], writes=[R_xres[tt]])

        def emit_fproj(dt):
            half = dt // 8
            ring = "A" if half == 0 else "B"
            rbuf = wringA if ring == "A" else wringB
            dt8 = dt % 8
            bq = dt % 4
            for kb in range(NKB):
                lh = rbuf[:, dt8 * 2048 + kb * 128: dt8 * 2048 + (kb + 1) * 128]
                mm(bq, lh, x_ln[:, kb * 512:(kb + 1) * 512], kb == 0, kb == NKB - 1,
                   [R_xln, R_ring[ring]])

        def attn():
            base_tag = p.cur_tag
            p.tag(base_tag + "V")
            for vb in range(4):
                ring = "A" if vb % 2 == 0 else "B"
                rbuf = wringA if ring == "A" else wringB
                dst = rbuf[:, 0:8192].rearrange("p (kb c) -> p kb c", c=512)
                src = wit[vb].rearrange("kb p c -> p kb c")
                p.emit("sync", lambda e, d=dst, s=src: e.dma_start(out=d, in_=s),
                       reads=[R_wi], writes=[R_ring[ring]], dma=True)
                for kb in range(NKB):
                    for tt in range(NTT):
                        lh = x_ln[:, kb * 512 + tt * 128: kb * 512 + (tt + 1) * 128]
                        mm(tt, lh, rbuf[:, kb * 512:(kb + 1) * 512], kb == 0, kb == NKB - 1,
                           [R_xln, R_ring[ring]])
                for tt in range(NTT):
                    dst = V_ap(tt * H + vb * 512, 512)
                    p.emit("vector", lambda e, d=dst, tt=tt: e.tensor_copy(d, pb[tt][:]),
                           reads=[R_bank[tt]], writes=[R_arena])
            p.tag(base_tag + "qs")
            for half in range(2):
                ring = "A" if half == 0 else "B"
                rbuf = wringA if ring == "A" else wringB
                dst = rbuf[:, 0:16384].rearrange("p (dt kb m) -> p dt kb m", kb=NKB, m=128)
                src = wqt[half * 8:(half + 1) * 8].rearrange("dt kb p m -> p dt kb m")
                p.emit("sync", lambda e, d=dst, s=src: e.dma_start(out=d, in_=s),
                       reads=[R_wq], writes=[R_ring[ring]], dma=True)
                for dt8 in range(8):
                    dt = half * 8 + dt8
                    bq = 2 + dt % 2
                    for kb in range(NKB):
                        lh = rbuf[:, dt8 * 2048 + kb * 128: dt8 * 2048 + (kb + 1) * 128]
                        mm(bq, lh, x_ln[:, kb * 512:(kb + 1) * 512], kb == 0, kb == NKB - 1,
                           [R_xln, R_ring[ring]])
                    p.emit("scalar", lambda e, dt=dt, b=bq: e.activation(
                        siluq_ap(dt * 512, 512), pb[b][:], AF.Silu),
                        reads=[R_bank[bq]], writes=[R_arena])
            p.tag(base_tag + "fr")
            def emit_list(ops):
                for (eng, fn, rd, wr) in ops:
                    p.emit(eng, fn, reads=rd, writes=wr)

            def interleave(a, b):
                out = []
                ia = ib = 0
                while ia < len(a) or ib < len(b):
                    ra = ia / max(len(a), 1)
                    rb = ib / max(len(b), 1)
                    if ib >= len(b) or (ia < len(a) and ra <= rb):
                        out.append(a[ia]); ia += 1
                    else:
                        out.append(b[ib]); ib += 1
                return out

            def elementwise_ops(dt):
                slot = dt % 2
                sc0 = 4 + slot * 30
                if slot == 0:
                    qtc, ktc, kbc = 512, 1024, 1536
                    qbuf = kbuf = bbuf = sbf
                    RQ = R_sbf
                else:
                    qtc, ktc, kbc = 0, 512, 1024
                    qbuf = kbuf = bbuf = hstage
                    RQ = R_hstage
                bq = dt % 4
                fps = pb[bq][:]
                o = []
                o.append(("scalar", lambda e, f=fps: e.activation(scr[:, 0:512], f, AF.Abs),
                          [R_bank[bq]], [R_scr]))
                o.append(("vector", lambda e, f=fps: e.tensor_scalar_min(scr[:, 1536:2048], f, 0.0),
                          [R_bank[bq]], [R_scr]))
                o.append(("scalar", lambda e: e.activation(scr[:, 512:1024], scr[:, 0:512], AF.Exp, scale=-1.0),
                          [R_scr], [R_scr]))
                o.append(("scalar", lambda e: e.activation(scr[:, 1024:1536], scr[:, 512:1024], AF.Ln, bias=1.0),
                          [R_scr], [R_scr]))
                o.append(("vector", lambda e: e.tensor_sub(scr[:, 2048:2560], scr[:, 1536:2048], scr[:, 1024:1536]),
                          [R_scr], [R_scr]))
                o.append(("scalar", lambda e: e.activation(scr[:, 0:512], scr[:, 2048:2560], AF.Exp),
                          [R_scr], [R_scr]))
                o.append(("vector", lambda e: e.tensor_scalar(scr[:, 512:1024], scr[:, 0:512], -1.0, 1.0, ALU.mult, ALU.add),
                          [R_scr], [R_scr]))
                for sg in range(4):
                    o.append(("vector", lambda e, sg=sg: e.tensor_tensor_scan(
                        scr[:, 2560 + sg * 128:2560 + (sg + 1) * 128], ones[:],
                        scr[:, 2048 + sg * 128:2048 + (sg + 1) * 128], 0.0, ALU.mult, ALU.add),
                        [R_scr, R_const], [R_scr]))
                bmids = bass.AP(scr, 2560 + 63, [[3072, 128], [128, 4]])
                bends = bass.AP(scr, 2560 + 127, [[3072, 128], [128, 4]])
                o.append(("vector", lambda e, v=bmids, c=sc0: e.tensor_copy(small[:, c:c + 4], v),
                          [R_scr], [R_small_s[slot]]))
                o.append(("vector", lambda e, v=bmids, c=sc0: e.tensor_scalar_mul(small[:, c + 24:c + 28], v, -1.0),
                          [R_scr], [R_small_s[slot]]))
                o.append(("vector", lambda e, v=bends, c=sc0: e.tensor_copy(small[:, c + 8:c + 12], v),
                          [R_scr], [R_small_s[slot]]))
                o.append(("vector", lambda e, a=bends, b2=bmids, c=sc0: e.tensor_tensor(
                    small[:, c + 4:c + 8], a, b2, ALU.subtract),
                    [R_scr], [R_small_s[slot]]))
                o.append(("scalar", lambda e, c=sc0: e.activation(small[:, c + 12:c + 24], small[:, c:c + 12], AF.Exp),
                          [R_small_s[slot]], [R_small_s[slot]]))
                for sg in range(4):
                    bmid = scr[:, 2560 + sg * 128 + 63:2560 + sg * 128 + 64]
                    o.append(("scalar", lambda e, sg=sg, c=sc0: e.activation(
                        scr[:, 0 + sg * 128:0 + (sg + 1) * 128],
                        scr[:, 2560 + sg * 128:2560 + (sg + 1) * 128], AF.Exp,
                        bias=small[:, c + 24 + sg:c + 25 + sg]),
                        [R_scr, R_small_s[slot]], [R_scr]))
                    o.append(("scalar", lambda e, sg=sg, bm=bmid: e.activation(
                        scr[:, 1536 + sg * 128:1536 + (sg + 1) * 128],
                        scr[:, 2560 + sg * 128:2560 + (sg + 1) * 128], AF.Exp,
                        scale=-1.0, bias=bm),
                        [R_scr], [R_scr]))
                o.append(("vector", lambda e, dt=dt, b2=qbuf, c=qtc: e.scalar_tensor_tensor(
                    b2[:, c:c + 512], siluq_ap(dt * 512, 512), QSCALE, scr[:, 0:512],
                    ALU.mult, ALU.mult), [R_arena, R_scr], [RQ]))
                o.append(("vector", lambda e, b2=kbuf, c=ktc: e.tensor_mul(
                    b2[:, c:c + 512], scr[:, 512:1024], scr[:, 1536:2048]),
                    [R_scr], [RQ]))
                for sg in range(4):
                    o.append(("vector", lambda e, sg=sg, b2=kbuf, ck=ktc, cb=kbc, c=sc0: e.tensor_scalar_mul(
                        b2[:, cb + sg * 128:cb + (sg + 1) * 128],
                        b2[:, ck + sg * 128:ck + (sg + 1) * 128], small[:, c + 16 + sg:c + 17 + sg]),
                        [RQ, R_small_s[slot]], [RQ]))
                return o

            def rec_ops(dt):
                slot = dt % 2
                sc0 = 4 + slot * 30
                if slot == 0:
                    qtc, ktc, kbc = 512, 1024, 1536
                    qbuf = sbf
                    RQ = R_sbf
                else:
                    qtc, ktc, kbc = 0, 512, 1024
                    qbuf = hstage
                    RQ = R_hstage
                o = []
                qt_ = [qbuf[:, qtc + sg * 128:qtc + (sg + 1) * 128] for sg in range(4)]
                kt_ = [qbuf[:, ktc + sg * 128:ktc + (sg + 1) * 128] for sg in range(4)]
                kb_ = [qbuf[:, kbc + sg * 128:kbc + (sg + 1) * 128] for sg in range(4)]
                vs_ = [V_ap(sg * H + dt * 128, 128) for sg in range(4)]
                sm_ = [smb[:, sg * 128:(sg + 1) * 128] for sg in range(4)]
                ktm_ = [smb[:, 512 + i * 128:512 + (i + 1) * 128] for i in range(2)]
                ht_ = [smb[:, 768 + i * 128:768 + (i + 1) * 128] for i in range(2)]
                for sg in range(4):
                    o.append(("tensor", lambda e, kt=kt_[sg], qt=qt_[sg], b=4 + sg: e.matmul(
                        pb[b][:, 0:128], kt, qt, start=True, stop=True),
                        [RQ], [R_bank[4 + sg]]))
                for sg in range(4):
                    o.append(("vector", lambda e, sg=sg, b=4 + sg: e.copy_predicated(
                        sm_[sg], masku[:], pb[b][:, 0:128]),
                        [R_bank[4 + sg], R_const], [R_sm[sg]]))
                for sg in range(4):
                    out_ps = pb[4 + sg][:, 0:64].bitcast(BF16)
                    o.append(("tensor", lambda e, o2=out_ps, i=kb_[sg]: e.transpose(o2, i, ident[:]),
                              [RQ, R_const], [R_bank[4 + sg]]))
                    o.append(("scalar", lambda e, o2=out_ps, k2=ktm_[sg % 2]: e.copy(k2, o2),
                              [R_bank[4 + sg]], [R_ktm[sg % 2]]))
                    o.append(("tensor", lambda e, k2=ktm_[sg % 2], v=vs_[sg], b=4 + sg: e.matmul(
                        pb[b][:, 0:128], k2, v, start=True, stop=True),
                        [R_ktm[sg % 2], R_arena], [R_bank[4 + sg]]))
                for sg in range(4):
                    c1 = sc0 + 12 + sg
                    c2 = sc0 + 20 + sg
                    o.append(("scalar", lambda e, dt=dt, c=c1, h2=ht_[sg % 2]: e.activation(
                        h2, Hst[:, dt * 128:(dt + 1) * 128], AF.Copy, scale=small[:, c:c + 1]),
                        [R_H[dt], R_small_s[slot]], [R_ht[sg % 2]]))
                    o.append(("vector", lambda e, dt=dt, c=c2, b=4 + sg: e.scalar_tensor_tensor(
                        Hst[:, dt * 128:(dt + 1) * 128], Hst[:, dt * 128:(dt + 1) * 128],
                        small[:, c:c + 1], pb[b][:, 0:128], ALU.mult, ALU.add),
                        [R_H[dt], R_small_s[slot], R_bank[4 + sg]], [R_H[dt]]))
                    o.append(("tensor", lambda e, sg=sg, v=vs_[sg], b=4 + sg: e.matmul(
                        pb[b][:, 0:128], sm_[sg], v, start=True, stop=False),
                        [R_sm[sg], R_arena], [R_bank[4 + sg]]))
                    o.append(("tensor", lambda e, qt=qt_[sg], h2=ht_[sg % 2], b=4 + sg: e.matmul(
                        pb[b][:, 0:128], qt, h2, start=False, stop=True),
                        [RQ, R_ht[sg % 2]], [R_bank[4 + sg]]))
                    dsto = o_ap(sg * H + dt * 128, 128)
                    o.append(("scalar", lambda e, d=dsto, b=4 + sg: e.copy(d, pb[b][:, 0:128]),
                              [R_bank[4 + sg]], [R_arena]))
                return o

            dstf = wringA[:, 0:16384].rearrange("p (dt kb m) -> p dt kb m", kb=NKB, m=128)
            srcf = wft[0:8].rearrange("dt kb p m -> p dt kb m")
            p.emit("sync", lambda e, d=dstf, s=srcf: e.dma_start(out=d, in_=s),
                   reads=[R_wf], writes=[R_ring["A"]], dma=True)
            emit_fproj(0)
            emit_fproj(1)
            dstf2 = wringB[:, 0:16384].rearrange("p (dt kb m) -> p dt kb m", kb=NKB, m=128)
            srcf2 = wft[8:16].rearrange("dt kb p m -> p dt kb m")
            p.emit("sync", lambda e, d=dstf2, s=srcf2: e.dma_start(out=d, in_=s),
                   reads=[R_wf], writes=[R_ring["B"]], dma=True)
            SEQ_DEBUG = SEQ_DEBUG_FLAG
            if SEQ_DEBUG:
                for dt in range(NH):
                    if dt + 2 < NH:
                        emit_fproj(dt + 2)
                    emit_list(elementwise_ops(dt))
                    emit_list(rec_ops(dt))
            else:
                for dt in range(NH):
                    if dt + 2 < NH:
                        emit_fproj(dt + 2)
                    ew = elementwise_ops(dt)
                    rc = rec_ops(dt - 1) if dt >= 1 else []
                    emit_list(interleave(ew, rc))
                emit_list(rec_ops(NH - 1))
            if STOP_AFTER == "rec":
                return
            _BC[0] += 1
            dstw0 = wringB[:, 0:16384].rearrange("p (kb c) -> p kb c", c=1024)
            p.emit("sync", lambda e, d=dstw0, s=wot[0].rearrange("kb p c -> p kb c"): e.dma_start(out=d, in_=s),
                   reads=[R_wo], writes=[R_ring["B"]], dma=True, batch=f"wo{_BC[0]}")
            dstw1 = wringA[:, 0:16384].rearrange("p (kb c) -> p kb c", c=1024)
            p.emit("sync", lambda e, d=dstw1, s=wot[1].rearrange("kb p c -> p kb c"): e.dma_start(out=d, in_=s),
                   reads=[R_wo], writes=[R_ring["A"]], dma=True, batch=f"wo{_BC[0]}")
            p.tag(base_tag + "rm")
            for tt in range(NTT):
                p.emit("scalar", lambda e, tt=tt: e.activation(
                    siluq_ap(0, 2048), o_ap(tt * H, H), AF.Square, accum_out=small[:, 0:1]),
                    reads=[R_arena], writes=[R_arena, R_small])
                p.emit("scalar", lambda e: e.activation(small[:, 1:2], small[:, 0:1], AF.Ln,
                                                        scale=1.0 / H, bias=EPS),
                       reads=[R_small], writes=[R_small])
                p.emit("scalar", lambda e: e.activation(small[:, 2:3], small[:, 1:2], AF.Exp, scale=-0.5),
                       reads=[R_small], writes=[R_small])
                p.emit("vector", lambda e, tt=tt: e.tensor_scalar_mul(
                    o_ap(tt * H, H), o_ap(tt * H, H), small[:, 2:3]),
                    reads=[R_arena, R_small], writes=[R_arena])
            for tt in range(NTT):
                for q4 in range(4):
                    grp = R_grp[q4 % 2]
                    outs = []
                    for j in range(4):
                        vk = q4 * 4 + j
                        outs.append((4 + j, tp(4 + j, o_ap(tt * H + vk * 128, 128),
                                               [R_arena], group=grp)))
                    for j in range(4):
                        vk = q4 * 4 + j
                        bk, o_ps = outs[j]
                        dst = hsp[:, vk * 512 + tt * 128: vk * 512 + (tt + 1) * 128]
                        p.emit("vector", lambda e, o=o_ps, d=dst: e.tensor_copy(d, o),
                               reads=[R_bank[bk], grp], writes=[R_arena])
            p.tag(base_tag + "wo")
            for dh in range(2):
                wring = wringB if dh == 0 else wringA
                Rw = R_ring["B"] if dh == 0 else R_ring["A"]
                for kb in range(NKB):
                    for tt in range(NTT):
                        lh = hsp[:, kb * 512 + tt * 128: kb * 512 + (tt + 1) * 128]
                        for dblk in range(2):
                            rh = wring[:, kb * 1024 + dblk * 512:
                                       kb * 1024 + (dblk + 1) * 512]
                            mm(tt * 2 + dblk, lh, rh, kb == 0, kb == NKB - 1,
                               [R_arena, Rw])
                for tt in range(NTT):
                    for dblk in range(2):
                        dst2 = x_res[:, tt * H + dh * 1024 + dblk * 512:
                                     tt * H + dh * 1024 + (dblk + 1) * 512]
                        bk = tt * 2 + dblk
                        p.emit("vector", lambda e, d=dst2, bk=bk: e.tensor_add(d, d, pb[bk][:]),
                               reads=[R_bank[bk], R_xres[tt]], writes=[R_xres[tt]])

        for ci in range(nchunk):
            p.tag(f"xload{ci}")
            for tt in range(NTT):
                r0 = ci * T + tt * 128
                p.emit("sync", lambda e, r0=r0, tt=tt: e.dma_start(
                    out=x_res[:, tt * H:(tt + 1) * H], in_=x_d[r0:r0 + 128, :]),
                    writes=[R_xres[tt]], dma=True, batch=f"xl{ci}")
            p.tag(f"ln1_{ci}")
            ln()
            p.tag(f"ffn1_{ci}")
            ffn()
            if STOP_AFTER != "ffn1":
                p.tag(f"ln2_{ci}")
                ln()
                p.tag(f"attn_{ci}")
                attn()
                if STOP_AFTER != "attn":
                    p.tag(f"ln3_{ci}")
                    ln()
                    p.tag(f"ffn2_{ci}")
                    ffn()
            p.tag(f"store{ci}")
            for tt in range(NTT):
                r0 = ci * T + tt * 128
                p.emit("scalar", lambda e, r0=r0, tt=tt: e.dma_start(
                    out=y_d[r0:r0 + 128, :], in_=x_res[:, tt * H:(tt + 1) * H]),
                    reads=[R_xres[tt]], writes=[R_y], dma=True, batch=f"ys{ci}")

        p.finalize(nc)
    return nc


_cached = {}


def kernel(L0_attn_x, L0_attn_Wq, L0_attn_Wf, L0_attn_Wi, L0_attn_Wo,
           L0_attn_g_weight, L0_ffn_w_gate, L0_ffn_w_up, L0_ffn_w_down):
    if "nc" not in _cached:
        _cached["nc"] = build_program()
    nc = _cached["nc"]
    x = np.ascontiguousarray(np.asarray(L0_attn_x, dtype=np.float32))
    common = {
        "wq": np.ascontiguousarray(np.asarray(L0_attn_Wq, np.float32)),
        "wf": np.ascontiguousarray(np.asarray(L0_attn_Wf, np.float32)),
        "wi": np.ascontiguousarray(np.asarray(L0_attn_Wi, np.float32)),
        "wo": np.ascontiguousarray(np.asarray(L0_attn_Wo, np.float32)),
        "gw": np.ascontiguousarray(np.asarray(L0_attn_g_weight, np.float32)),
        "wg": np.ascontiguousarray(np.asarray(L0_ffn_w_gate, np.float32)),
        "wu": np.ascontiguousarray(np.asarray(L0_ffn_w_up, np.float32)),
        "wd": np.ascontiguousarray(np.asarray(L0_ffn_w_down, np.float32)),
    }
    in_maps = [dict(common, x=np.ascontiguousarray(x[b])) for b in range(B)]
    res = run_bass_kernel_spmd(nc, in_maps, core_ids=list(range(B)))
    out = np.stack([np.asarray(res.results[b]["y"]) for b in range(B)], axis=0)
    return out.astype(np.float32)


# version 08
# speedup vs baseline: 1.0639x; 1.0639x over previous
import numpy as np
import ml_dtypes
import concourse.bass as bass
import concourse.mybir as mybir
from concourse.bass_utils import run_bass_kernel_spmd

F32 = mybir.dt.float32
BF16 = mybir.dt.bfloat16
U8 = mybir.dt.uint8
AF = mybir.ActivationFunctionType
ALU = mybir.AluOpType

B, S, H = 8, 4096, 2048
_BC = [0]
_CI = [0]


def _q2():
    return "gpsimd" if _CI[0] >= 2 else "sync"
I = 8192
NH, DK = 16, 128
T = 512
NCHUNK = S // T
NTT = T // 128
NKB = H // 128
NIT = I // 128
NI16 = 16
EPS = 1e-5
QSCALE = float(DK) ** -0.5
SEQ_DEBUG_FLAG = False
STOP_AFTER = None

ENGINES = ("sync", "scalar", "gpsimd", "vector", "tensor")
DMA_ENG = ("sync", "scalar", "gpsimd")


class Res:
    __slots__ = ("name", "writers", "readers", "frozen")

    def __init__(self, name, frozen=False):
        self.name = name
        self.writers = {}
        self.readers = {}
        self.frozen = frozen


class Prog:

    def __init__(self):
        self.items = {e: [] for e in ENGINES}
        self.needed = {e: set() for e in ENGINES}
        self.cur_tag = "init"

    def tag(self, t):
        self.cur_tag = t

    def emit(self, engine, fn, reads=(), writes=(), dma=False, batch=None):
        idx = len(self.items[engine])
        st = ("q:" + engine) if dma else engine
        deps = {}

        def add(s2, i2):
            if s2 == st and (dma or engine == "tensor"):
                return
            if i2 > deps.get(s2, -1):
                deps[s2] = i2

        for r in reads:
            for s2, i2 in r.writers.items():
                add(s2, i2)
        for w in writes:
            for s2, i2 in w.writers.items():
                add(s2, i2)
            for s2, i2 in w.readers.items():
                add(s2, i2)
        for r in reads:
            if not r.frozen and r.readers.get(st, -1) < idx:
                r.readers[st] = idx
        for w in writes:
            w.writers[st] = idx
            w.readers = {}
        self.items[engine].append((fn, deps, dma, self.cur_tag, batch))

    def finalize(self, nc):
        dma_index = {}
        for e in ENGINES:
            seq = 0
            di = []
            for (fn, deps, dma, tg, batch) in self.items[e]:
                di.append(seq if dma else -1)
                if dma:
                    seq += 1
            dma_index[e] = di
        batch_end = {}
        for e in ENGINES:
            items = self.items[e]
            ends = [-1] * len(items)
            groups = []
            for i, (fn, deps, dma, tg, batch) in enumerate(items):
                if not dma:
                    continue
                if groups and batch is not None and groups[-1][0] is not None and groups[-1][0] == batch:
                    groups[-1][1].append(i)
                else:
                    groups.append([batch, [i]])
            for key, idxs in groups:
                endseq = dma_index[e][idxs[-1]]
                for i in idxs:
                    ends[i] = endseq
            batch_end[e] = ends
        for e in ENGINES:
            for (fn, deps, dma, tg, batch) in self.items[e]:
                for s2, i2 in deps.items():
                    if not s2.startswith("q:"):
                        self.needed[s2].add(i2)
        mstone = {}
        for e in ENGINES:
            for n, i in enumerate(sorted(self.needed[e])):
                mstone[(e, i)] = n + 1
        total_cmp = {e: len(self.needed[e]) for e in ENGINES}
        total_dma = {e: sum(1 for d in dma_index[e] if d >= 0) * 16 for e in ENGINES}
        for e in ENGINES:
            assert total_cmp[e] < 60000, (e, total_cmp[e])

        with nc.Block() as block:
            sems = {}
            for e in ENGINES:
                sems[("cmp", e)] = nc.semaphore(f"p_{e}").__enter__()
                if e in DMA_ENG:
                    sems[("dma", e)] = nc.semaphore(f"q_{e}").__enter__()

            def make(e):
                def emitf(eng):
                    floors = {}
                    for idx, (fn, deps, dma, tg, batch) in enumerate(self.items[e]):
                        if dma and dma_index[e][idx] >= 1:
                            myseq = dma_index[e][idx]
                            myend = batch_end[e][idx]
                            if not hasattr(self, "_seq_end"):
                                self._seq_end = {}
                            se = self._seq_end.get(e)
                            if se is None:
                                se = [None] * (sum(1 for d in dma_index[e] if d >= 0))
                                for j, (fn2, d2, dm2, tg2, b2) in enumerate(self.items[e]):
                                    if dm2:
                                        se[dma_index[e][j]] = batch_end[e][j]
                                self._seq_end[e] = se
                            sstart = myseq
                            while sstart > 0 and se[sstart - 1] == myend:
                                sstart -= 1
                            if sstart >= 1:
                                key, val = ("dma", e), sstart * 16
                                if floors.get(key, 0) < val:
                                    eng.wait_ge(sems[key], val)
                                    floors[key] = val
                        for s2 in sorted(deps):
                            i2 = deps[s2]
                            if s2.startswith("q:"):
                                e2 = s2[2:]
                                key, val = ("dma", e2), (batch_end[e2][i2] + 1) * 16
                            else:
                                key, val = ("cmp", s2), mstone[(s2, i2)]
                            if floors.get(key, 0) < val:
                                eng.wait_ge(sems[key], val)
                                floors[key] = val
                        with nc.named_scope(tg):
                            ins = fn(eng)
                        if dma:
                            ins.then_inc(sems[("dma", e)], 16)
                        elif idx in self.needed[e]:
                            ins.then_inc(sems[("cmp", e)], 1)
                    if e == "sync":
                        for e2 in ENGINES:
                            if total_cmp[e2]:
                                eng.wait_ge(sems[("cmp", e2)], total_cmp[e2])
                            if e2 in DMA_ENG and total_dma[e2]:
                                eng.wait_ge(sems[("dma", e2)], total_dma[e2])
                return emitf

            block.sync(make("sync"))
            block.scalar(make("scalar"))
            block.gpsimd(make("gpsimd"))
            block.vector(make("vector"))
            block.tensor(make("tensor"))


def build_program(nchunk=NCHUNK):
    nc = bass.Bass("TRN2", target_bir_lowering=False)

    x_d = nc.dram_tensor("x", [S, H], F32, kind="ExternalInput")
    wq_d = nc.dram_tensor("wq", [H, H], F32, kind="ExternalInput")
    wf_d = nc.dram_tensor("wf", [H, H], F32, kind="ExternalInput")
    wi_d = nc.dram_tensor("wi", [H, H], F32, kind="ExternalInput")
    wo_d = nc.dram_tensor("wo", [H, H], F32, kind="ExternalInput")
    gw_d = nc.dram_tensor("gw", [H], F32, kind="ExternalInput")
    wg_d = nc.dram_tensor("wg", [H, I], F32, kind="ExternalInput")
    wu_d = nc.dram_tensor("wu", [H, I], F32, kind="ExternalInput")
    wd_d = nc.dram_tensor("wd", [I, H], F32, kind="ExternalInput")
    y_d = nc.dram_tensor("y", [S, H], F32, kind="ExternalOutput")

    wgt = nc.dram_tensor("wgt", [NI16, 128, 8192], BF16)
    wut = nc.dram_tensor("wut", [NI16, 128, 8192], BF16)
    wdt = nc.dram_tensor("wdt", [8, 128, 16384], BF16)
    wqt = nc.dram_tensor("wqt", [2, 128, 16384], BF16)
    wft = nc.dram_tensor("wft", [2, 128, 16384], BF16)
    wit = nc.dram_tensor("wit", [4, 128, 8192], BF16)
    wot = nc.dram_tensor("wot", [2, 128, 16384], BF16)

    ident_d = nc.inline_tensor(np.eye(128).astype(ml_dtypes.bfloat16), name="ident_c")
    mask_np = (np.arange(128)[:, None] <= np.arange(128)[None, :]).astype(np.uint8)
    mask_d = nc.inline_tensor(mask_np, name="maskc_c")

    p = Prog()
    from contextlib import ExitStack
    stack = ExitStack()
    with stack:
        sbt = lambda name, shape, dt: stack.enter_context(nc.sbuf_tensor(name, shape, dt))
        x_res = sbt("x_res", [128, NTT * H], F32)
        x_ln = sbt("x_ln", [128, NKB * 512], BF16)
        Hst = sbt("Hst", [128, NH * 128], F32)
        wringA = sbt("wringA", [128, 16384], BF16)
        wringB = sbt("wringB", [128, 16384], BF16)
        hsp = sbt("hsp", [128, 32768], BF16)
        scr = sbt("scr", [128, 3072], F32)
        sbf = sbt("sbf", [128, 2048], BF16)
        smb = sbt("smb", [128, 1024], BF16)
        hstage = sbt("hstage", [128, 2048], BF16)
        ident = sbt("ident", [128, 128], BF16)
        masku = sbt("masku", [128, 128], U8)
        zerosb = sbt("zerosb", [128, 128], BF16)
        ones = sbt("ones", [128, 128], F32)
        small = sbt("small", [128, 64], F32)
        bnscr = sbt("bnscr", [128, 32], F32)
        gwcol = sbt("gwcol", [128, NKB], F32)
        constb = sbt("constb", [128, 2], F32)
        pb = [stack.enter_context(nc.psum_tensor(f"pb{i}", [128, 512], F32)) for i in range(8)]

        arena32 = hsp[:].bitcast(F32)

        def siluq_ap(c0, n):
            return arena32[:, c0:c0 + n]

        def V_ap(c0, n):
            return hsp[:, 16384 + c0:16384 + c0 + n]

        def o_ap(c0, n):
            return hsp[:, 24576 + c0:24576 + c0 + n]

        woscr = arena32[:, 14336:16384]
        wosbf = hsp[:, 26624:28672]

        R_bank = [Res(f"bank{i}") for i in range(8)]
        R_xres = [Res(f"xres{t}") for t in range(NTT)]
        R_xln = Res("xln")
        R_arena = Res("arena")
        R_H = [Res(f"H{h}") for h in range(NH)]
        R_ring = {"A": Res("ringA"), "B": Res("ringB")}
        R_scr = Res("scr")
        R_sbf = Res("sbf")
        R_smb = Res("smb")
        R_sm = [Res(f"sm{i}") for i in range(4)]
        R_ktm = [Res(f"ktm{i}") for i in range(2)]
        R_ht = [Res(f"ht{i}") for i in range(2)]
        R_hstage = Res("hstage")
        R_small = Res("small")
        R_small_s = [Res("small0"), Res("small1")]
        R_bn = Res("bn")
        R_const = Res("const", frozen=True)
        R_wg = [Res(f"wg{i}") for i in range(NI16)]
        R_wu = [Res(f"wu{i}") for i in range(NI16)]
        R_wd = [Res(f"wd{i}") for i in range(8)]
        R_wq = Res("wqt")
        R_wf = Res("wft")
        R_wi = Res("wit")
        R_wo = Res("wot")
        R_gw = Res("gw")
        R_y = Res("y")

        p.emit("scalar", lambda e: e.dma_start(out=ident[:], in_=ident_d[:]),
               writes=[R_const], dma=True)
        p.emit("scalar", lambda e: e.dma_start(out=masku[:], in_=mask_d[:]),
               writes=[R_const], dma=True)
        p.emit("vector", lambda e: e.memset(zerosb[:], 0.0), writes=[R_const])
        for sg4 in range(4):
            p.emit("vector", lambda e, sg4=sg4: e.memset(smb[:, sg4 * 128:(sg4 + 1) * 128], 0.0),
                   writes=[R_sm[sg4]])
        p.emit("vector", lambda e: e.memset(constb[:, 0:1], EPS), writes=[R_const])
        p.emit("vector", lambda e: e.memset(constb[:, 1:2], 1.0), writes=[R_const])
        nc.const_aps.register(F32, EPS, constb[:, 0:1])
        nc.const_aps.register(F32, 1.0, constb[:, 1:2])
        p.emit("vector", lambda e: e.memset(ones[:], 1.0), writes=[R_const])
        p.emit("vector", lambda e: e.memset(Hst[:], 0.0), writes=list(R_H))
        for kb in range(NKB):
            src = bass.AP(gw_d, kb * 128, [[1, 128], [0, 1]])
            p.emit("scalar", lambda e, s=src, kb=kb: e.dma_start(
                out=gwcol[:, kb:kb + 1], in_=s, allow_slow_non_contiguous=True),
                writes=[R_gw], dma=True)

        wg4 = wg_d[:].rearrange("(kb p) (ib c) -> ib p kb c", p=128, c=512)
        wu4 = wu_d[:].rearrange("(kb p) (ib c) -> ib p kb c", p=128, c=512)
        for ib in range(NI16):
            dg = wgt[ib].rearrange("p (kb c) -> p kb c", c=512)
            p.emit("gpsimd", lambda e, d=dg, ib=ib: e.dma_start(out=d, in_=wg4[ib]),
                   writes=[R_wg[ib]], dma=True, batch=f"pcgu{ib // 4}")
            du = wut[ib].rearrange("p (kb c) -> p kb c", c=512)
            p.emit("gpsimd", lambda e, d=du, ib=ib: e.dma_start(out=d, in_=wu4[ib]),
                   writes=[R_wu[ib]], dma=True, batch=f"pcgu{ib // 4}")
        wd3 = wd_d[:].rearrange("(run it p) h -> run p it h", p=128, it=8)
        for run in range(8):
            dd = wdt[run].rearrange("p (it h) -> p it h", h=H)
            p.emit("gpsimd", lambda e, d=dd, run=run: e.dma_start(
                out=d, in_=wd3[run]),
                writes=[R_wd[run]], dma=True, batch=f"pcwd{run // 4}")
        wq5 = wq_d[:].rearrange("(kb p) (hf dtm) -> hf p kb dtm", p=128, dtm=1024)
        wf5 = wf_d[:].rearrange("(kb p) (hf dtm) -> hf p kb dtm", p=128, dtm=1024)
        wi4 = wi_d[:].rearrange("(kb p) (vb c) -> vb p kb c", p=128, c=512)
        for hf in range(2):
            dq = wqt[hf].rearrange("p (kb dtm) -> p kb dtm", dtm=1024)
            p.emit("gpsimd", lambda e, d=dq, s=wq5[hf]: e.dma_start(out=d, in_=s),
                   writes=[R_wq], dma=True, batch=f"pca{hf}")
            df = wft[hf].rearrange("p (kb dtm) -> p kb dtm", dtm=1024)
            p.emit("gpsimd", lambda e, d=df, s=wf5[hf]: e.dma_start(out=d, in_=s),
                   writes=[R_wf], dma=True, batch=f"pca{hf}")
        for vb in range(4):
            di = wit[vb].rearrange("p (kb c) -> p kb c", c=512)
            p.emit("gpsimd", lambda e, d=di, s=wi4[vb]: e.dma_start(out=d, in_=s),
                   writes=[R_wi], dma=True, batch=f"pci{vb // 2}")
        for kb in range(NKB):
            p.emit("scalar", lambda e, kb=kb: e.dma_start(
                out=woscr, in_=wo_d[kb * 128:(kb + 1) * 128, :]),
                reads=[R_arena], writes=[R_arena], dma=True)
            p.emit("vector", lambda e, kb=kb: e.tensor_scalar_mul(
                wosbf, woscr, gwcol[:, kb:kb + 1]),
                reads=[R_arena, R_gw], writes=[R_arena])
            for dh in range(2):
                src = wosbf[:, dh * 1024:(dh + 1) * 1024]
                dsto = wot[dh][:, kb * 1024:(kb + 1) * 1024]
                p.emit("scalar", lambda e, s=src, d=dsto: e.dma_start(out=d, in_=s),
                       reads=[R_arena], writes=[R_wo], dma=True)

        def mm(bank, lhsT, rhs, start, stop, reads, group=None):
            n = rhs.shape[-1]
            w = [R_bank[bank]] + ([group] if group is not None else [])
            p.emit("tensor", lambda e, b=bank, l=lhsT, r=rhs, st=start, sp=stop, n=n:
                   e.matmul(pb[b][:, 0:n], l, r, start=st, stop=sp),
                   reads=list(reads), writes=w)

        def tp(bank, in_ap, reads, group=None):
            out = pb[bank][:, 0:64].bitcast(BF16)
            w = [R_bank[bank]] + ([group] if group is not None else [])
            p.emit("tensor", lambda e, o=out, i=in_ap: e.transpose(o, i, ident[:]),
                   reads=list(reads) + [R_const], writes=w)
            return out

        R_grp = [Res(f"grp{i}") for i in range(4)]

        def ln():
            for tt in range(NTT):
                xs = x_res[:, tt * H:(tt + 1) * H]
                for q in range(4):
                    p.emit("vector", lambda e, q=q, xs=xs: e.bn_stats(
                        bnscr[:, q * 6:q * 6 + 6], xs[:, q * 512:(q + 1) * 512]),
                        reads=[R_xres[tt]], writes=[R_bn])
                p.emit("vector", lambda e: e.bn_aggr(bnscr[:, 24:26], bnscr[:, 0:24]),
                       reads=[R_bn], writes=[R_bn])
                p.emit("scalar", lambda e: e.activation(small[:, 0:1], bnscr[:, 25:26], AF.Ln, bias=EPS),
                       reads=[R_bn], writes=[R_small])
                p.emit("scalar", lambda e: e.activation(small[:, 1:2], small[:, 0:1], AF.Exp, scale=-0.5),
                       reads=[R_small], writes=[R_small])
                p.emit("vector", lambda e, xs=xs: e.tensor_scalar(
                    sbf[:], xs, bnscr[:, 24:25], small[:, 1:2], ALU.subtract, ALU.mult),
                    reads=[R_xres[tt], R_bn, R_small], writes=[R_sbf])
                for q4 in range(4):
                    grp = R_grp[q4 % 2]
                    outs = []
                    for j in range(4):
                        kb = q4 * 4 + j
                        outs.append((4 + j, tp(4 + j, sbf[:, kb * 128:(kb + 1) * 128],
                                               [R_sbf], group=grp)))
                    for j in range(4):
                        kb = q4 * 4 + j
                        bk, o_ps = outs[j]
                        dst = x_ln[:, kb * 512 + tt * 128: kb * 512 + (tt + 1) * 128]
                        p.emit("vector", lambda e, o=o_ps, d=dst: e.tensor_copy(d, o),
                               reads=[R_bank[bk], grp], writes=[R_xln])

        def ffn():
            pend = []

            def flush_tp(bankset):
                nonlocal pend
                for (par, tt, i16) in pend:
                    grp = R_grp[2 + (tt % 2)]
                    outs = []
                    for j2 in range(4):
                        bk = bankset[j2 % 4]
                        src = hstage[:, par * 1024 + (tt % 2) * 512 + j2 * 128:
                                     par * 1024 + (tt % 2) * 512 + (j2 + 1) * 128]
                        outs.append((bk, tp(bk, src, [R_hstage], group=grp)))
                    for j2 in range(4):
                        it = i16 * 4 + j2
                        bk, o_ps = outs[j2]
                        dst = hsp[:, it * 512 + tt * 128: it * 512 + (tt + 1) * 128]
                        p.emit("vector", lambda e, o=o_ps, d=dst: e.tensor_copy(d, o),
                               reads=[R_bank[bk], grp], writes=[R_arena])
                pend = []

            it_count = 0
            base_tag = p.cur_tag
            p.tag(base_tag + "gu")
            for i16 in range(NI16):
                ring = "A" if i16 % 2 == 0 else "B"
                gbuf = wringA if ring == "A" else wringB
                _BC[0] += 1
                p.emit("sync", lambda e, d=gbuf[:, 0:8192], s=wgt[i16]: e.dma_start(out=d, in_=s),
                       reads=[R_wg[i16]], writes=[R_ring[ring]], dma=True, batch=f"gu{_BC[0]}")
                p.emit(_q2(), lambda e, d=gbuf[:, 8192:16384], s=wut[i16]: e.dma_start(out=d, in_=s),
                       reads=[R_wu[i16]], writes=[R_ring[ring]], dma=True, batch=f"gu{_BC[0]}")
                for pair in range(2):
                    par = it_count % 2
                    bs = [0, 1, 2, 3] if par == 0 else [4, 5, 6, 7]
                    obs = [4, 5, 6, 7] if par == 0 else [0, 1, 2, 3]
                    tts = (2 * pair, 2 * pair + 1)
                    pgrp = R_grp[par]
                    for kb in range(NKB):
                        for j, tt in enumerate(tts):
                            lh = x_ln[:, kb * 512 + tt * 128: kb * 512 + (tt + 1) * 128]
                            mm(bs[j], lh, gbuf[:, kb * 512:(kb + 1) * 512],
                               kb == 0, kb == NKB - 1, [R_xln, R_ring[ring]], group=pgrp)
                            mm(bs[2 + j], lh, gbuf[:, 8192 + kb * 512:8192 + (kb + 1) * 512],
                               kb == 0, kb == NKB - 1, [R_xln, R_ring[ring]], group=pgrp)
                    flush_tp(obs)
                    for j, tt in enumerate(tts):
                        p.emit("scalar", lambda e, b=bs[j]: e.activation(
                            scr[:, 0:512], pb[b][:], AF.Silu),
                            reads=[R_bank[bs[j]], pgrp], writes=[R_scr])
                        p.emit("vector", lambda e, b=bs[2 + j]: e.tensor_mul(
                            scr[:, 512:1024], scr[:, 0:512], pb[b][:]),
                            reads=[R_bank[bs[2 + j]], R_scr], writes=[R_scr])
                        hh = scr[:, 512:1024]
                        p.emit("scalar", lambda e: e.activation(scr[:, 1024:1536], hh, AF.Abs),
                               reads=[R_scr], writes=[R_scr])
                        a2 = scr[:, 1024:1536].rearrange("p (g f) -> p g f", f=2)
                        p.emit("vector", lambda e, a2=a2: e.tensor_tensor(
                            scr[:, 1536:1792], a2[:, :, 0], a2[:, :, 1], ALU.max),
                            reads=[R_scr], writes=[R_scr])
                        p.emit("vector", lambda e, a2=a2: e.tensor_tensor(
                            scr[:, 1792:2048], a2[:, :, 0], a2[:, :, 1], ALU.min),
                            reads=[R_scr], writes=[R_scr])
                        mx2 = scr[:, 1536:1792].rearrange("p (g f) -> p g f", f=2)
                        mn2 = scr[:, 1792:2048].rearrange("p (g f) -> p g f", f=2)
                        p.emit("vector", lambda e, v=mx2: e.tensor_tensor(
                            scr[:, 2048:2176], v[:, :, 0], v[:, :, 1], ALU.min),
                            reads=[R_scr], writes=[R_scr])
                        p.emit("vector", lambda e, v=mn2: e.tensor_tensor(
                            scr[:, 2176:2304], v[:, :, 0], v[:, :, 1], ALU.max),
                            reads=[R_scr], writes=[R_scr])
                        p.emit("vector", lambda e: e.tensor_tensor(
                            scr[:, 2304:2432], scr[:, 2048:2176], scr[:, 2176:2304], ALU.max),
                            reads=[R_scr], writes=[R_scr])
                        a3 = scr[:, 1024:1536].rearrange("p (g f) -> p g f", f=4)
                        g3 = scr[:, 2560:3072].rearrange("p (g f) -> p g f", f=4)
                        for m4 in range(4):
                            p.emit("vector", lambda e, m4=m4, a3=a3, g3=g3: e.tensor_tensor(
                                g3[:, :, m4], a3[:, :, m4], scr[:, 2304:2432], ALU.is_ge),
                                reads=[R_scr], writes=[R_scr])
                        dsth = hstage[:, par * 1024 + (tt % 2) * 512: par * 1024 + (tt % 2) * 512 + 512]
                        p.emit("vector", lambda e, hh=hh, d=dsth: e.scalar_tensor_tensor(
                            d, hh, 0.5, scr[:, 2560:3072], ALU.mult, ALU.mult),
                            reads=[R_scr], writes=[R_hstage])
                        pend.append((par, tt, i16))
                    it_count += 1
            flush_tp([0, 1, 2, 3])
            p.tag(base_tag + "dn")
            for dh in range(2):
                for run in range(8):
                    ring = "A" if run % 2 == 0 else "B"
                    rbuf = wringA if ring == "A" else wringB
                    qe = "sync" if run % 2 == 0 else _q2()
                    p.emit(qe, lambda e, d=rbuf[:, 0:8 * H], s=wdt[run]: e.dma_start(out=d, in_=s),
                           reads=[R_wd[run]], writes=[R_ring[ring]], dma=True)
                    for kk in range(8):
                        it = run * 8 + kk
                        for tt in range(NTT):
                            lh = hsp[:, it * 512 + tt * 128: it * 512 + (tt + 1) * 128]
                            for dblk in range(2):
                                rh = rbuf[:, kk * H + dh * 1024 + dblk * 512:
                                          kk * H + dh * 1024 + (dblk + 1) * 512]
                                mm(tt * 2 + dblk, lh, rh, it == 0, it == NIT - 1,
                                   [R_arena, R_ring[ring]])
                for tt in range(NTT):
                    for dblk in range(2):
                        dst = x_res[:, tt * H + dh * 1024 + dblk * 512:
                                    tt * H + dh * 1024 + (dblk + 1) * 512]
                        bk = tt * 2 + dblk
                        p.emit("vector", lambda e, d=dst, bk=bk: e.tensor_add(d, d, pb[bk][:]),
                               reads=[R_bank[bk], R_xres[tt]], writes=[R_xres[tt]])

        def emit_fproj(dt):
            half = dt // 8
            ring = "A" if half == 0 else "B"
            rbuf = wringA if ring == "A" else wringB
            dt8 = dt % 8
            bq = dt % 4
            for kb in range(NKB):
                lh = rbuf[:, kb * 1024 + dt8 * 128: kb * 1024 + (dt8 + 1) * 128]
                mm(bq, lh, x_ln[:, kb * 512:(kb + 1) * 512], kb == 0, kb == NKB - 1,
                   [R_xln, R_ring[ring]])

        def attn():
            base_tag = p.cur_tag
            p.tag(base_tag + "V")
            for vb in range(4):
                ring = "A" if vb % 2 == 0 else "B"
                rbuf = wringA if ring == "A" else wringB
                qe = "sync" if vb % 2 == 0 else _q2()
                p.emit(qe, lambda e, d=rbuf[:, 0:8192], s=wit[vb]: e.dma_start(out=d, in_=s),
                       reads=[R_wi], writes=[R_ring[ring]], dma=True)
                for kb in range(NKB):
                    for tt in range(NTT):
                        lh = x_ln[:, kb * 512 + tt * 128: kb * 512 + (tt + 1) * 128]
                        mm(tt, lh, rbuf[:, kb * 512:(kb + 1) * 512], kb == 0, kb == NKB - 1,
                           [R_xln, R_ring[ring]])
                for tt in range(NTT):
                    dst = V_ap(tt * H + vb * 512, 512)
                    p.emit("vector", lambda e, d=dst, tt=tt: e.tensor_copy(d, pb[tt][:]),
                           reads=[R_bank[tt]], writes=[R_arena])
            p.tag(base_tag + "qs")
            for half in range(2):
                ring = "A" if half == 0 else "B"
                rbuf = wringA if ring == "A" else wringB
                qe = "sync" if half == 0 else _q2()
                p.emit(qe, lambda e, d=rbuf[:, 0:16384], s=wqt[half]: e.dma_start(out=d, in_=s),
                       reads=[R_wq], writes=[R_ring[ring]], dma=True)
                for dt8 in range(8):
                    dt = half * 8 + dt8
                    bq = 2 + dt % 2
                    for kb in range(NKB):
                        lh = rbuf[:, kb * 1024 + dt8 * 128: kb * 1024 + (dt8 + 1) * 128]
                        mm(bq, lh, x_ln[:, kb * 512:(kb + 1) * 512], kb == 0, kb == NKB - 1,
                           [R_xln, R_ring[ring]])
                    p.emit("scalar", lambda e, dt=dt, b=bq: e.activation(
                        siluq_ap(dt * 512, 512), pb[b][:], AF.Silu),
                        reads=[R_bank[bq]], writes=[R_arena])
            p.tag(base_tag + "fr")
            def emit_list(ops):
                for (eng, fn, rd, wr) in ops:
                    p.emit(eng, fn, reads=rd, writes=wr)

            def interleave(a, b):
                out = []
                ia = ib = 0
                while ia < len(a) or ib < len(b):
                    ra = ia / max(len(a), 1)
                    rb = ib / max(len(b), 1)
                    if ib >= len(b) or (ia < len(a) and ra <= rb):
                        out.append(a[ia]); ia += 1
                    else:
                        out.append(b[ib]); ib += 1
                return out

            def elementwise_ops(dt):
                slot = dt % 2
                sc0 = 4 + slot * 30
                if slot == 0:
                    qtc, ktc, kbc = 512, 1024, 1536
                    qbuf = kbuf = bbuf = sbf
                    RQ = R_sbf
                else:
                    qtc, ktc, kbc = 0, 512, 1024
                    qbuf = kbuf = bbuf = hstage
                    RQ = R_hstage
                bq = dt % 4
                fps = pb[bq][:]
                o = []
                o.append(("scalar", lambda e, f=fps: e.activation(scr[:, 0:512], f, AF.Abs),
                          [R_bank[bq]], [R_scr]))
                o.append(("vector", lambda e, f=fps: e.tensor_scalar_min(scr[:, 1536:2048], f, 0.0),
                          [R_bank[bq]], [R_scr]))
                o.append(("scalar", lambda e: e.activation(scr[:, 512:1024], scr[:, 0:512], AF.Exp, scale=-1.0),
                          [R_scr], [R_scr]))
                o.append(("scalar", lambda e: e.activation(scr[:, 1024:1536], scr[:, 512:1024], AF.Ln, bias=1.0),
                          [R_scr], [R_scr]))
                o.append(("vector", lambda e: e.tensor_sub(scr[:, 2048:2560], scr[:, 1536:2048], scr[:, 1024:1536]),
                          [R_scr], [R_scr]))
                o.append(("scalar", lambda e: e.activation(scr[:, 0:512], scr[:, 2048:2560], AF.Exp),
                          [R_scr], [R_scr]))
                o.append(("vector", lambda e: e.tensor_scalar(scr[:, 512:1024], scr[:, 0:512], -1.0, 1.0, ALU.mult, ALU.add),
                          [R_scr], [R_scr]))
                for sg in range(4):
                    o.append(("vector", lambda e, sg=sg: e.tensor_tensor_scan(
                        scr[:, 2560 + sg * 128:2560 + (sg + 1) * 128], ones[:],
                        scr[:, 2048 + sg * 128:2048 + (sg + 1) * 128], 0.0, ALU.mult, ALU.add),
                        [R_scr, R_const], [R_scr]))
                bmids = bass.AP(scr, 2560 + 63, [[3072, 128], [128, 4]])
                bends = bass.AP(scr, 2560 + 127, [[3072, 128], [128, 4]])
                o.append(("vector", lambda e, v=bmids, c=sc0: e.tensor_copy(small[:, c:c + 4], v),
                          [R_scr], [R_small_s[slot]]))
                o.append(("vector", lambda e, v=bmids, c=sc0: e.tensor_scalar_mul(small[:, c + 24:c + 28], v, -1.0),
                          [R_scr], [R_small_s[slot]]))
                o.append(("vector", lambda e, v=bends, c=sc0: e.tensor_copy(small[:, c + 8:c + 12], v),
                          [R_scr], [R_small_s[slot]]))
                o.append(("vector", lambda e, a=bends, b2=bmids, c=sc0: e.tensor_tensor(
                    small[:, c + 4:c + 8], a, b2, ALU.subtract),
                    [R_scr], [R_small_s[slot]]))
                o.append(("scalar", lambda e, c=sc0: e.activation(small[:, c + 12:c + 24], small[:, c:c + 12], AF.Exp),
                          [R_small_s[slot]], [R_small_s[slot]]))
                for sg in range(4):
                    bmid = scr[:, 2560 + sg * 128 + 63:2560 + sg * 128 + 64]
                    o.append(("scalar", lambda e, sg=sg, c=sc0: e.activation(
                        scr[:, 0 + sg * 128:0 + (sg + 1) * 128],
                        scr[:, 2560 + sg * 128:2560 + (sg + 1) * 128], AF.Exp,
                        bias=small[:, c + 24 + sg:c + 25 + sg]),
                        [R_scr, R_small_s[slot]], [R_scr]))
                    o.append(("scalar", lambda e, sg=sg, bm=bmid: e.activation(
                        scr[:, 1536 + sg * 128:1536 + (sg + 1) * 128],
                        scr[:, 2560 + sg * 128:2560 + (sg + 1) * 128], AF.Exp,
                        scale=-1.0, bias=bm),
                        [R_scr], [R_scr]))
                o.append(("vector", lambda e, dt=dt, b2=qbuf, c=qtc: e.scalar_tensor_tensor(
                    b2[:, c:c + 512], siluq_ap(dt * 512, 512), QSCALE, scr[:, 0:512],
                    ALU.mult, ALU.mult), [R_arena, R_scr], [RQ]))
                o.append(("vector", lambda e, b2=kbuf, c=ktc: e.tensor_mul(
                    b2[:, c:c + 512], scr[:, 512:1024], scr[:, 1536:2048]),
                    [R_scr], [RQ]))
                for sg in range(4):
                    o.append(("vector", lambda e, sg=sg, b2=kbuf, ck=ktc, cb=kbc, c=sc0: e.tensor_scalar_mul(
                        b2[:, cb + sg * 128:cb + (sg + 1) * 128],
                        b2[:, ck + sg * 128:ck + (sg + 1) * 128], small[:, c + 16 + sg:c + 17 + sg]),
                        [RQ, R_small_s[slot]], [RQ]))
                return o

            def rec_ops(dt):
                slot = dt % 2
                sc0 = 4 + slot * 30
                if slot == 0:
                    qtc, ktc, kbc = 512, 1024, 1536
                    qbuf = sbf
                    RQ = R_sbf
                else:
                    qtc, ktc, kbc = 0, 512, 1024
                    qbuf = hstage
                    RQ = R_hstage
                o = []
                qt_ = [qbuf[:, qtc + sg * 128:qtc + (sg + 1) * 128] for sg in range(4)]
                kt_ = [qbuf[:, ktc + sg * 128:ktc + (sg + 1) * 128] for sg in range(4)]
                kb_ = [qbuf[:, kbc + sg * 128:kbc + (sg + 1) * 128] for sg in range(4)]
                vs_ = [V_ap(sg * H + dt * 128, 128) for sg in range(4)]
                sm_ = [smb[:, sg * 128:(sg + 1) * 128] for sg in range(4)]
                ktm_ = [smb[:, 512 + i * 128:512 + (i + 1) * 128] for i in range(2)]
                ht_ = [smb[:, 768 + i * 128:768 + (i + 1) * 128] for i in range(2)]
                for sg in range(4):
                    o.append(("tensor", lambda e, kt=kt_[sg], qt=qt_[sg], b=4 + sg: e.matmul(
                        pb[b][:, 0:128], kt, qt, start=True, stop=True),
                        [RQ], [R_bank[4 + sg]]))
                for sg in range(4):
                    o.append(("vector", lambda e, sg=sg, b=4 + sg: e.copy_predicated(
                        sm_[sg], masku[:], pb[b][:, 0:128]),
                        [R_bank[4 + sg], R_const], [R_sm[sg]]))
                for sg in range(4):
                    out_ps = pb[4 + sg][:, 0:64].bitcast(BF16)
                    o.append(("tensor", lambda e, o2=out_ps, i=kb_[sg]: e.transpose(o2, i, ident[:]),
                              [RQ, R_const], [R_bank[4 + sg]]))
                    o.append(("scalar", lambda e, o2=out_ps, k2=ktm_[sg % 2]: e.copy(k2, o2),
                              [R_bank[4 + sg]], [R_ktm[sg % 2]]))
                    o.append(("tensor", lambda e, k2=ktm_[sg % 2], v=vs_[sg], b=4 + sg: e.matmul(
                        pb[b][:, 0:128], k2, v, start=True, stop=True),
                        [R_ktm[sg % 2], R_arena], [R_bank[4 + sg]]))
                for sg in range(4):
                    c1 = sc0 + 12 + sg
                    c2 = sc0 + 20 + sg
                    o.append(("scalar", lambda e, dt=dt, c=c1, h2=ht_[sg % 2]: e.activation(
                        h2, Hst[:, dt * 128:(dt + 1) * 128], AF.Copy, scale=small[:, c:c + 1]),
                        [R_H[dt], R_small_s[slot]], [R_ht[sg % 2]]))
                    o.append(("vector", lambda e, dt=dt, c=c2, b=4 + sg: e.scalar_tensor_tensor(
                        Hst[:, dt * 128:(dt + 1) * 128], Hst[:, dt * 128:(dt + 1) * 128],
                        small[:, c:c + 1], pb[b][:, 0:128], ALU.mult, ALU.add),
                        [R_H[dt], R_small_s[slot], R_bank[4 + sg]], [R_H[dt]]))
                    o.append(("tensor", lambda e, sg=sg, v=vs_[sg], b=4 + sg: e.matmul(
                        pb[b][:, 0:128], sm_[sg], v, start=True, stop=False),
                        [R_sm[sg], R_arena], [R_bank[4 + sg]]))
                    o.append(("tensor", lambda e, qt=qt_[sg], h2=ht_[sg % 2], b=4 + sg: e.matmul(
                        pb[b][:, 0:128], qt, h2, start=False, stop=True),
                        [RQ, R_ht[sg % 2]], [R_bank[4 + sg]]))
                    dsto = o_ap(sg * H + dt * 128, 128)
                    o.append(("scalar", lambda e, d=dsto, b=4 + sg: e.copy(d, pb[b][:, 0:128]),
                              [R_bank[4 + sg]], [R_arena]))
                return o

            p.emit("sync", lambda e, d=wringA[:, 0:16384], s=wft[0]: e.dma_start(out=d, in_=s),
                   reads=[R_wf], writes=[R_ring["A"]], dma=True)
            emit_fproj(0)
            emit_fproj(1)
            p.emit(_q2(), lambda e, d=wringB[:, 0:16384], s=wft[1]: e.dma_start(out=d, in_=s),
                   reads=[R_wf], writes=[R_ring["B"]], dma=True)
            SEQ_DEBUG = SEQ_DEBUG_FLAG
            if SEQ_DEBUG:
                for dt in range(NH):
                    if dt + 2 < NH:
                        emit_fproj(dt + 2)
                    emit_list(elementwise_ops(dt))
                    emit_list(rec_ops(dt))
            else:
                for dt in range(NH):
                    if dt + 2 < NH:
                        emit_fproj(dt + 2)
                    ew = elementwise_ops(dt)
                    rc = rec_ops(dt - 1) if dt >= 1 else []
                    emit_list(interleave(ew, rc))
                emit_list(rec_ops(NH - 1))
            if STOP_AFTER == "rec":
                return
            _BC[0] += 1
            p.emit("sync", lambda e, d=wringB[:, 0:16384], s=wot[0]: e.dma_start(out=d, in_=s),
                   reads=[R_wo], writes=[R_ring["B"]], dma=True, batch=f"wo{_BC[0]}")
            p.emit(_q2(), lambda e, d=wringA[:, 0:16384], s=wot[1]: e.dma_start(out=d, in_=s),
                   reads=[R_wo], writes=[R_ring["A"]], dma=True, batch=f"wo{_BC[0]}")
            p.tag(base_tag + "rm")
            for tt in range(NTT):
                p.emit("scalar", lambda e, tt=tt: e.activation(
                    siluq_ap(0, 2048), o_ap(tt * H, H), AF.Square, accum_out=small[:, 0:1]),
                    reads=[R_arena], writes=[R_arena, R_small])
                p.emit("scalar", lambda e: e.activation(small[:, 1:2], small[:, 0:1], AF.Ln,
                                                        scale=1.0 / H, bias=EPS),
                       reads=[R_small], writes=[R_small])
                p.emit("scalar", lambda e: e.activation(small[:, 2:3], small[:, 1:2], AF.Exp, scale=-0.5),
                       reads=[R_small], writes=[R_small])
                p.emit("vector", lambda e, tt=tt: e.tensor_scalar_mul(
                    o_ap(tt * H, H), o_ap(tt * H, H), small[:, 2:3]),
                    reads=[R_arena, R_small], writes=[R_arena])
            for tt in range(NTT):
                for q4 in range(4):
                    grp = R_grp[q4 % 2]
                    outs = []
                    for j in range(4):
                        vk = q4 * 4 + j
                        outs.append((4 + j, tp(4 + j, o_ap(tt * H + vk * 128, 128),
                                               [R_arena], group=grp)))
                    for j in range(4):
                        vk = q4 * 4 + j
                        bk, o_ps = outs[j]
                        dst = hsp[:, vk * 512 + tt * 128: vk * 512 + (tt + 1) * 128]
                        p.emit("vector", lambda e, o=o_ps, d=dst: e.tensor_copy(d, o),
                               reads=[R_bank[bk], grp], writes=[R_arena])
            p.tag(base_tag + "wo")
            for dh in range(2):
                wring = wringB if dh == 0 else wringA
                Rw = R_ring["B"] if dh == 0 else R_ring["A"]
                for kb in range(NKB):
                    for tt in range(NTT):
                        lh = hsp[:, kb * 512 + tt * 128: kb * 512 + (tt + 1) * 128]
                        for dblk in range(2):
                            rh = wring[:, kb * 1024 + dblk * 512:
                                       kb * 1024 + (dblk + 1) * 512]
                            mm(tt * 2 + dblk, lh, rh, kb == 0, kb == NKB - 1,
                               [R_arena, Rw])
                for tt in range(NTT):
                    for dblk in range(2):
                        dst2 = x_res[:, tt * H + dh * 1024 + dblk * 512:
                                     tt * H + dh * 1024 + (dblk + 1) * 512]
                        bk = tt * 2 + dblk
                        p.emit("vector", lambda e, d=dst2, bk=bk: e.tensor_add(d, d, pb[bk][:]),
                               reads=[R_bank[bk], R_xres[tt]], writes=[R_xres[tt]])

        for ci in range(nchunk):
            _CI[0] = ci
            p.tag(f"xload{ci}")
            for tt in range(NTT):
                r0 = ci * T + tt * 128
                p.emit("sync", lambda e, r0=r0, tt=tt: e.dma_start(
                    out=x_res[:, tt * H:(tt + 1) * H], in_=x_d[r0:r0 + 128, :]),
                    writes=[R_xres[tt]], dma=True, batch=f"xl{ci}")
            p.tag(f"ln1_{ci}")
            ln()
            p.tag(f"ffn1_{ci}")
            ffn()
            if STOP_AFTER != "ffn1":
                p.tag(f"ln2_{ci}")
                ln()
                p.tag(f"attn_{ci}")
                attn()
                if STOP_AFTER != "attn":
                    p.tag(f"ln3_{ci}")
                    ln()
                    p.tag(f"ffn2_{ci}")
                    ffn()
            p.tag(f"store{ci}")
            for tt in range(NTT):
                r0 = ci * T + tt * 128
                p.emit("scalar", lambda e, r0=r0, tt=tt: e.dma_start(
                    out=y_d[r0:r0 + 128, :], in_=x_res[:, tt * H:(tt + 1) * H]),
                    reads=[R_xres[tt]], writes=[R_y], dma=True, batch=f"ys{ci}")

        p.finalize(nc)
    return nc


_cached = {}


def kernel(L0_attn_x, L0_attn_Wq, L0_attn_Wf, L0_attn_Wi, L0_attn_Wo,
           L0_attn_g_weight, L0_ffn_w_gate, L0_ffn_w_up, L0_ffn_w_down):
    if "nc" not in _cached:
        _cached["nc"] = build_program()
    nc = _cached["nc"]
    x = np.ascontiguousarray(np.asarray(L0_attn_x, dtype=np.float32))
    common = {
        "wq": np.ascontiguousarray(np.asarray(L0_attn_Wq, np.float32)),
        "wf": np.ascontiguousarray(np.asarray(L0_attn_Wf, np.float32)),
        "wi": np.ascontiguousarray(np.asarray(L0_attn_Wi, np.float32)),
        "wo": np.ascontiguousarray(np.asarray(L0_attn_Wo, np.float32)),
        "gw": np.ascontiguousarray(np.asarray(L0_attn_g_weight, np.float32)),
        "wg": np.ascontiguousarray(np.asarray(L0_ffn_w_gate, np.float32)),
        "wu": np.ascontiguousarray(np.asarray(L0_ffn_w_up, np.float32)),
        "wd": np.ascontiguousarray(np.asarray(L0_ffn_w_down, np.float32)),
    }
    in_maps = [dict(common, x=np.ascontiguousarray(x[b])) for b in range(B)]
    res = run_bass_kernel_spmd(nc, in_maps, core_ids=list(range(B)))
    out = np.stack([np.asarray(res.results[b]["y"]) for b in range(B)], axis=0)
    return out.astype(np.float32)
